# version 05 (final)
# baseline (speedup 1.0000x reference)
import contextlib
import numpy as np
import ml_dtypes
import concourse.bass as bass
import concourse.mybir as mybir
from concourse.bass_utils import run_bass_kernel_spmd
from concourse.bass_core_rust import IndirectOffsetOnAxis

F32 = mybir.dt.float32
BF16 = mybir.dt.bfloat16
I32 = mybir.dt.int32
AF = mybir.ActivationFunctionType
ALU = mybir.AluOpType
AX = mybir.AxisListType
BF = ml_dtypes.bfloat16

NCORES = 8
TPC = 2048
D = 1024
NEG = -1.0e30


ENG_NAMES = {"t": "tensor", "v": "vector", "s": "scalar", "g": "gpsimd", "y": "sync"}


class Chan:
    def __init__(self, sem, inc):
        self.sem = sem
        self.inc = inc
        self.count = 0
        self.pending = None


class K:
    def __init__(self, name):
        self.nc = bass.Bass("TRN2", target_bir_lowering=False, debug=False)
        self.name = name
        self.ops = []
        self.chans = {}
        self.last_wait = {}
        self.es = contextlib.ExitStack()
        self._consts = {}

    def sbuf(self, name, shape, dtype):
        return self.es.enter_context(self.nc.sbuf_tensor(name, shape, dtype))

    def psum(self, name, shape, dtype=F32):
        return self.es.enter_context(self.nc.psum_tensor(name, shape, dtype))

    def setup_chans(self):
        for key in ["t", "v", "s", "g"]:
            sem = self.es.enter_context(self.nc.semaphore("c_" + key))
            self.chans[key] = Chan(sem, 1)
        for key in ["dy", "dg", "ds", "dv"]:
            sem = self.es.enter_context(self.nc.semaphore("c_" + key))
            self.chans[key] = Chan(sem, 16)

    def on(self, key, fn):
        rec = {"incs": [], "fn": fn, "key": key}
        self.ops.append(rec)
        self.chans[key].pending = rec
        return rec

    def dma(self, key, fn):
        ch = self.chans["d" + key]
        rec = {"incs": [(ch, 16)], "fn": fn, "key": key}
        self.ops.append(rec)
        ch.count += 16
        return rec

    def dma_i(self, key, fn):
        ch = self.chans["d" + key]
        rec = {"incs": [(ch, 16)], "fn": fn, "key": key}
        self.ops.append(rec)
        ch.count += 16
        return rec

    def wait(self, waiter, *chan_keys):
        for ck in chan_keys:
            ch = self.chans[ck]
            if ch.pending is not None:
                ch.pending["incs"].append((ch, ch.inc))
                ch.count += ch.inc
                ch.pending = None
            n = ch.count
            if n == 0:
                continue
            if self.last_wait.get((waiter, ck), -1) >= n:
                continue
            self.last_wait[(waiter, ck)] = n
            sem = ch.sem
            self.ops.append({"incs": [], "key": waiter,
                             "fn": lambda e, sem=sem, n=n: e.wait_ge(sem, n)})

    def wait_all_on(self, waiter):
        self.wait(waiter, "t", "v", "s", "g", "dy", "dg", "ds", "dv")

    def drain(self, key):
        self.ops.append({"incs": [], "key": key, "fn": lambda e: e.drain()})

    def finalize(self):
        nc = self.nc
        with self.es:
            with nc.Block() as block:
                for key, ename in ENG_NAMES.items():
                    def section(eng, key=key):
                        for rec in self.ops:
                            if rec["key"] != key:
                                continue
                            instr = rec["fn"](eng)
                            for (ch, inc) in rec["incs"]:
                                instr.then_inc(ch.sem, inc)
                    getattr(block, ename)(section)
        return nc



def emit_const_tile(k, name, shape, value, dtype=F32):
    t = k.sbuf(name, shape, dtype)
    k.on("g", lambda e: e.memset(t[:], value))
    return t


def emit_identity(k, name="ident"):
    t = k.sbuf(name, [128, 128], BF16)
    k.on("g", lambda e: e.memset(t[:], 0.0))
    k.on("g", lambda e: e.affine_select(out=t[:], in_=t[:], compare_op=ALU.not_equal,
                                        fill=1.0, base=0, pattern=[[-1, 128]],
                                        channel_multiplier=1))
    return t


def emit_ones_row(k, name="onesrow"):
    t = k.sbuf(name, [1, 128], BF16)
    k.on("g", lambda e: e.memset(t[:], 1.0))
    return t


def emit_row_bcast(k, ones_row, psum_b, dst, src_row, scale=None):
    npad = src_row.shape[-1]
    k.wait("t", "v", "s", "g", "dy", "dg")
    k.on("t", lambda e: e.matmul(psum_b[:, 0:npad], ones_row[:, 0:128], src_row,
                                 start=True, stop=True))
    if dst is not None:
        k.wait("v", "t")
        if scale is None:
            k.on("v", lambda e: e.tensor_copy(dst[:, 0:npad], psum_b[:, 0:npad]))
        else:
            k.on("v", lambda e: e.tensor_scalar_mul(dst[:, 0:npad], psum_b[:, 0:npad], scale))


def emit_ln_tile(k, x_tile, out_bf, sc, eps=1e-5, dmodel=1024):
    k.on("s", lambda e: e.activation(sc["sq"][:], x_tile, AF.Square, accum_out=sc["s2"][:]))
    k.on("v", lambda e: e.reduce_sum(sc["s1"][:], x_tile, axis=AX.X))
    k.wait("v", "s")
    k.drain("v")
    k.on("v", lambda e: e.tensor_scalar_mul(sc["m"][:], sc["s1"][:], 1.0 / dmodel))
    k.on("v", lambda e: e.tensor_scalar_mul(sc["var"][:], sc["s2"][:], 1.0 / dmodel))
    k.drain("v")
    k.on("v", lambda e: e.tensor_mul(sc["m2"][:], sc["m"][:], sc["m"][:]))
    k.drain("v")
    k.on("v", lambda e: e.tensor_sub(sc["var"][:], sc["var"][:], sc["m2"][:]))
    k.wait("s", "v")
    k.on("s", lambda e: e.activation(sc["std"][:], sc["var"][:], AF.Sqrt, bias=sc["epst"][:]))
    k.wait("v", "s")
    k.on("v", lambda e: e.reciprocal(sc["rstd"][:], sc["std"][:]))
    k.drain("v")
    k.on("v", lambda e: e.tensor_scalar(out_bf, x_tile, sc["m"][:], sc["rstd"][:],
                                        ALU.subtract, ALU.mult))


def make_ln_scratch(k, prefix, dmodel=1024):
    sc = {}
    sc["sq"] = k.sbuf(prefix + "_sq", [128, dmodel], F32)
    for n in ["s1", "s2", "m", "m2", "var", "std", "rstd", "negb"]:
        sc[n] = k.sbuf(prefix + "_" + n, [128, 1], F32)
    sc["epst"] = emit_const_tile(k, prefix + "_eps", [128, 1], 1e-5)
    return sc


def emit_transpose_128(k, ident, psum_tr, src_bf, dst_bf):
    k.on("t", lambda e: e.transpose(psum_tr, src_bf, ident[:]))
    k.wait("v", "t")
    k.on("v", lambda e: e.tensor_copy(dst_bf, psum_tr))


def emit_cast_stream(k, dst_bf, src_f32, eng="g"):
    k.on(eng, lambda e: e.tensor_copy(dst_bf, src_f32))



def build_A1(debug=False):
    k = K("A1")
    nc = k.nc
    x = nc.dram_tensor("x", [TPC, D], F32, kind="ExternalInput").ap()
    w = nc.dram_tensor("w", [D, 3072], F32, kind="ExternalInput").ap()
    qt = nc.dram_tensor("qt", [D, TPC], BF16, kind="ExternalOutput").ap()
    kt = nc.dram_tensor("kt", [D, TPC], BF16, kind="ExternalOutput").ap()
    vo = nc.dram_tensor("vo", [TPC, D], BF16, kind="ExternalOutput").ap()
    if debug:
        lnT_dbg = nc.dram_tensor("lnT_dbg", [D, TPC], BF16, kind="ExternalOutput").ap()

    k.setup_chans()
    xt = k.sbuf("xt", [128, 2, D], F32)
    lnb = k.sbuf("lnb", [128, D], BF16)
    lnT = k.sbuf("lnT", [128, 8, TPC], BF16)
    wb = k.sbuf("wb", [128, 8, 3072], BF16)
    wstage = k.sbuf("wstage", [128, 2, 8, 128], F32)
    outb = k.sbuf("outb", [128, 2, 512], BF16)
    ident = emit_identity(k)
    lnsc = make_ln_scratch(k, "ln")
    ptr = k.psum("ptr", [128, 128], BF16)
    pmm = k.psum("pmm", [128, 2, 512], F32)

    for ci in range(24):
        buf = ci % 2
        k.wait("y", "g")
        k.dma("y", lambda e, ci=ci, buf=buf: e.dma_start(
            out=wstage[:, buf],
            in_=w[:, ci * 128:(ci + 1) * 128].rearrange("(c p) f -> p c f", p=128)))
        k.wait("g", "dy")
        k.on("g", lambda e, ci=ci, buf=buf: e.tensor_copy(
            wb[:, :, ci * 128:(ci + 1) * 128], wstage[:, buf]))

    for ti in range(16):
        buf = ti % 2
        k.wait("y", "s", "v")
        k.dma("y", lambda e, ti=ti, buf=buf: e.dma_start(out=xt[:, buf], in_=x[ti * 128:(ti + 1) * 128, :]))
        k.wait("v", "dy")
        k.wait("s", "dy")
        k.wait("v", "t")
        emit_ln_tile(k, xt[:, buf], lnb[:], lnsc)
        for ch in range(8):
            k.wait("t", "s", "v")
            k.on("t", lambda e, ch=ch: e.transpose(ptr[:], lnb[:, ch * 128:(ch + 1) * 128], ident[:]))
            k.wait("v", "t")
            k.on("v", lambda e, ti=ti, ch=ch: e.tensor_copy(
                lnT[:, ch, ti * 128:(ti + 1) * 128], ptr[:]))

    if debug:
        k.wait("g", "v")
        k.dma("g", lambda e: e.dma_start(out=lnT_dbg.rearrange("(c p) t -> p c t", p=128), in_=lnT[:]))

    k.wait("t", "g", "v")
    for fi in range(16):
        for tj in range(4):
            buf = (fi * 4 + tj) % 2
            k.wait("t", "s")
            for ch in range(8):
                k.on("t", lambda e, fi=fi, tj=tj, ch=ch, buf=buf: e.matmul(
                    pmm[:, buf], wb[:, ch, fi * 128:(fi + 1) * 128],
                    lnT[:, ch, tj * 512:(tj + 1) * 512],
                    start=(ch == 0), stop=(ch == 7)))
            k.wait("s", "t", "dg")
            scale = (1.0 / 32.0) if fi < 8 else 1.0
            k.on("s", lambda e, buf=buf, scale=scale: e.activation(
                outb[:, buf], pmm[:, buf], AF.Copy, scale=scale))
            dst = qt if fi < 8 else kt
            fo = fi if fi < 8 else fi - 8
            k.wait("g", "s")
            k.dma("g", lambda e, dst=dst, fo=fo, tj=tj, buf=buf: e.dma_start(
                out=dst[fo * 128:(fo + 1) * 128, tj * 512:(tj + 1) * 512], in_=outb[:, buf]))

    for ti in range(16):
        for fh in range(2):
            buf = (ti * 2 + fh) % 2
            k.wait("t", "s")
            for ch in range(8):
                k.on("t", lambda e, ti=ti, fh=fh, ch=ch, buf=buf: e.matmul(
                    pmm[:, buf], lnT[:, ch, ti * 128:(ti + 1) * 128],
                    wb[:, ch, 2048 + fh * 512:2048 + (fh + 1) * 512],
                    start=(ch == 0), stop=(ch == 7)))
            k.wait("s", "t", "dg")
            k.on("s", lambda e, buf=buf: e.activation(outb[:, buf], pmm[:, buf], AF.Copy))
            k.wait("g", "s")
            k.dma("g", lambda e, ti=ti, fh=fh, buf=buf: e.dma_start(
                out=vo[ti * 128:(ti + 1) * 128, fh * 512:(fh + 1) * 512], in_=outb[:, buf]))

    k.wait("g", "dg")
    return k.finalize()



def build_A2():
    k = K("A2")
    nc = k.nc
    x = nc.dram_tensor("x", [TPC, D], F32, kind="ExternalInput").ap()
    qt = nc.dram_tensor("qt", [D, TPC], BF16, kind="ExternalInput").ap()
    ktf = nc.dram_tensor("ktf", [D, 8192], BF16, kind="ExternalInput").ap()
    vf = nc.dram_tensor("vf", [8192, D], BF16, kind="ExternalInput").ap()
    mdin = nc.dram_tensor("maskd", [128, 512], F32, kind="ExternalInput").ap()
    x1 = nc.dram_tensor("x1", [TPC, D], F32, kind="ExternalOutput").ap()

    k.setup_chans()
    qtS = k.sbuf("qtS", [128, 8, TPC], BF16)
    ktS = k.sbuf("ktS", [128, 2, 8, 512], BF16)
    vS = k.sbuf("vS", [128, 2, 4, D], BF16)
    ss = k.sbuf("ss", [128, 8192], F32)
    pb = k.sbuf("pb", [128, 8192], BF16)
    pT = k.sbuf("pT", [128, 2, 4, 128], BF16)
    xt = k.sbuf("xt", [128, D], F32)
    o1 = k.sbuf("o1", [128, D], F32)
    m = k.sbuf("m", [128, 1], F32)
    negm = k.sbuf("negm", [128, 1], F32)
    l = k.sbuf("l", [128, 1], F32)
    linv = k.sbuf("linv", [128, 1], F32)
    maskd = k.sbuf("maskd_sb", [128, 512], F32)
    ident = emit_identity(k)
    psc = k.psum("psc", [128, 2, 512], F32)
    ptr = k.psum("ptr", [128, 128], BF16)
    pav = k.psum("pav", [128, 2, 512], F32)

    k.dma("y", lambda e: e.dma_start(out=maskd[:], in_=mdin))
    k.dma("y", lambda e: e.dma_start(out=qtS[:], in_=qt.rearrange("(c p) t -> p c t", p=128)))

    for i in range(16):
        for kb in range(i, 16):
            kbuf = kb % 2
            sbuf_ = kb % 2
            k.wait("y", "t")
            k.dma("y", lambda e, kb=kb, kbuf=kbuf: e.dma_start(
                out=ktS[:, kbuf], in_=ktf[:, kb * 512:(kb + 1) * 512]
                .rearrange("(c p) t -> p c t", p=128)))
            k.wait("t", "dy", "v", "s")
            for ch in range(8):
                k.on("t", lambda e, i=i, kb=kb, ch=ch, sbuf_=sbuf_: e.matmul(
                    psc[:, sbuf_], qtS[:, ch, i * 128:(i + 1) * 128],
                    ktS[:, kb % 2, ch, :], start=(ch == 0), stop=(ch == 7)))
            if kb == i:
                k.wait("v", "t", "dy")
                k.on("v", lambda e, i=i, sbuf_=sbuf_: e.scalar_tensor_tensor(
                    ss[:, i * 512:(i + 1) * 512], psc[:, sbuf_], 1.0, maskd[:],
                    ALU.mult, ALU.add))
            else:
                k.wait("s", "t")
                k.on("s", lambda e, kb=kb, sbuf_=sbuf_: e.activation(
                    ss[:, kb * 512:(kb + 1) * 512], psc[:, sbuf_], AF.Copy))
        w0 = i * 512
        k.wait("v", "s")
        k.on("v", lambda e, w0=w0: e.reduce_max(m[:], ss[:, w0:8192], axis=AX.X))
        k.drain("v")
        k.on("v", lambda e: e.tensor_scalar_mul(negm[:], m[:], -1.0))
        k.wait("s", "v")
        k.on("s", lambda e, w0=w0: e.activation(pb[:, w0:8192], ss[:, w0:8192], AF.Exp,
                                                bias=negm[:], accum_out=l[:]))
        k.wait("v", "s")
        k.on("v", lambda e: e.reciprocal(linv[:], l[:]))
        k.wait("y", "v")
        k.dma("y", lambda e, i=i: e.dma_start(out=xt[:], in_=x[i * 128:(i + 1) * 128, :]))
        for kb in range(i, 16):
            vbuf = kb % 2
            tbuf = kb % 2
            k.wait("y", "t")
            k.dma("y", lambda e, kb=kb, vbuf=vbuf: e.dma_start(
                out=vS[:, vbuf], in_=vf[kb * 512:(kb + 1) * 512, :]
                .rearrange("(c p) d -> p c d", p=128)))
            for sub in range(4):
                k.wait("t", "v", "s")
                k.on("t", lambda e, kb=kb, sub=sub: e.transpose(
                    ptr[:], pb[:, kb * 512 + sub * 128: kb * 512 + (sub + 1) * 128], ident[:]))
                k.wait("v", "t")
                k.on("v", lambda e, tbuf=tbuf, sub=sub: e.tensor_copy(pT[:, tbuf, sub], ptr[:]))
            k.wait("t", "v", "dy")
            for dh in range(2):
                for sub in range(4):
                    k.on("t", lambda e, kb=kb, i=i, dh=dh, sub=sub, vbuf=vbuf, tbuf=tbuf: e.matmul(
                        pav[:, dh], pT[:, tbuf, sub], vS[:, vbuf, sub, dh * 512:(dh + 1) * 512],
                        start=(kb == i and sub == 0), stop=(kb == 15 and sub == 3)))
        k.wait("v", "t", "dy")
        k.drain("v")
        for dh in range(2):
            k.on("v", lambda e, dh=dh: e.scalar_tensor_tensor(
                o1[:, dh * 512:(dh + 1) * 512], pav[:, dh], linv[:],
                xt[:, dh * 512:(dh + 1) * 512], ALU.mult, ALU.add))
        k.wait("g", "v")
        k.dma("g", lambda e, i=i: e.dma_start(out=x1[i * 128:(i + 1) * 128, :], in_=o1[:]))

    k.wait("g", "dg")
    return k.finalize()



def build_F1():
    k = K("F1")
    nc = k.nc
    x1 = nc.dram_tensor("x1", [TPC, D], F32, kind="ExternalInput").ap()
    w13c = nc.dram_tensor("w13c", [17, D, D], F32, kind="ExternalInput").ap()
    w2c = nc.dram_tensor("w2c", [17, 512, D], F32, kind="ExternalInput").ap()
    gwT = nc.dram_tensor("gwT", [D, 16], F32, kind="ExternalInput").ap()
    gb = nc.dram_tensor("gb", [1, 16], F32, kind="ExternalInput").ap()
    x2 = nc.dram_tensor("x2", [TPC, D], F32, kind="ExternalOutput").ap()
    ctDR = nc.dram_tensor("ctDR", [32, TPC], BF16, kind="Internal").ap()

    k.setup_chans()
    x2s = k.sbuf("x2s", [128, 16, D], F32)
    lnb = k.sbuf("lnb", [128, D], BF16)
    lnT = k.sbuf("lnT", [128, 8, TPC], BF16)
    wst = k.sbuf("wst", [128, 1, 8, 128], F32)
    wb13 = k.sbuf("wb13", [128, 8, D], BF16)
    wb2 = k.sbuf("wb2", [128, 2, 4, D], BF16)
    h2 = k.sbuf("h2", [128, 2, 4, TPC], BF16)
    g4 = k.sbuf("g4", [128, 4, TPC], BF16)
    cb2 = k.sbuf("cb2", [128, 2, TPC], BF16)
    wgb = k.sbuf("wgb", [128, 8, 16], BF16)
    gbt = k.sbuf("gbt", [1, 16], F32)
    gbB = k.sbuf("gbB", [128, 16], F32)
    onesf = k.sbuf("onesf", [1, 128], F32)
    combA = k.sbuf("combA", [128, 16, 17], F32)
    ctSB = k.sbuf("ctSB", [32, TPC], BF16)
    cbrow = k.sbuf("cbrow", [1, TPC], BF16)
    onesb = k.sbuf("onesb", [1, 128], BF16)
    scor = k.sbuf("scor", [128, 16], F32)
    bias_ = k.sbuf("bias_", [128, 16], F32)
    cur = k.sbuf("cur", [128, 16], F32)
    msk = k.sbuf("msk", [128, 16], F32)
    th = k.sbuf("th", [128, 1], F32)
    wsel = k.sbuf("wsel", [128, 16], F32)
    wsum = k.sbuf("wsum", [128, 1], F32)
    winv = k.sbuf("winv", [128, 1], F32)
    identf = k.sbuf("identf", [128, 128], F32)
    lnsc = make_ln_scratch(k, "ln")
    p1 = k.psum("p1", [128, 2, 512], F32)
    p2 = k.psum("p2", [128, 2, 512], F32)
    pbc = k.psum("pbc", [128, 512], F32)
    pr = k.psum("pr", [128, 16], F32)
    pct = k.psum("pct", [32, 128], F32)

    k.on("g", lambda e: e.memset(onesf[:], 1.0))
    k.on("g", lambda e: e.memset(onesb[:], 1.0))
    k.on("g", lambda e: e.memset(identf[:], 0.0))
    k.on("g", lambda e: e.affine_select(out=identf[:], in_=identf[:], compare_op=ALU.not_equal,
                                        fill=1.0, base=0, pattern=[[-1, 128]],
                                        channel_multiplier=1))
    k.on("g", lambda e: e.memset(combA[:, :, 16:17], 1.0))
    k.dma("y", lambda e: e.dma_start(out=gbt[:], in_=gb))

    k.dma("y", lambda e: e.dma_start(out=wst[:, 0, :, 0:16],
                                     in_=gwT.rearrange("(c p) f -> p c f", p=128)))
    k.wait("g", "dy")
    k.on("g", lambda e: e.tensor_copy(wgb[:], wst[:, 0, :, 0:16]))

    k.wait("t", "dy", "g")
    k.on("t", lambda e: e.matmul(pr[:], onesf[:], gbt[:], start=True, stop=True))
    k.wait("v", "t")
    k.on("v", lambda e: e.tensor_copy(gbB[:], pr[:]))

    for ti in range(16):
        k.dma("y", lambda e, ti=ti: e.dma_start(out=x2s[:, ti], in_=x1[ti * 128:(ti + 1) * 128, :]))
    k.wait("v", "dy")
    k.wait("s", "dy")
    ptr = k.psum("ptr", [128, 128], BF16)
    ident = emit_identity(k)
    for ti in range(16):
        k.wait("v", "t")
        emit_ln_tile(k, x2s[:, ti], lnb[:], lnsc)
        for ch in range(8):
            k.wait("t", "v")
            k.on("t", lambda e, ch=ch: e.transpose(ptr[:], lnb[:, ch * 128:(ch + 1) * 128], ident[:]))
            k.wait("v", "t")
            k.on("v", lambda e, ti=ti, ch=ch: e.tensor_copy(lnT[:, ch, ti * 128:(ti + 1) * 128], ptr[:]))

    for ti in range(16):
        k.wait("t", "v", "s", "g")
        for ch in range(8):
            k.on("t", lambda e, ti=ti, ch=ch: e.matmul(pr[:], lnT[:, ch, ti * 128:(ti + 1) * 128],
                                                       wgb[:, ch, :], start=(ch == 0), stop=(ch == 7)))
        k.wait("s", "t")
        k.on("s", lambda e: e.activation(scor[:], pr[:], AF.Sigmoid))
        k.wait("v", "s")
        k.on("v", lambda e: e.tensor_add(bias_[:], scor[:], gbB[:]))
        k.drain("v")
        k.on("v", lambda e: e.tensor_copy(cur[:], bias_[:]))
        for r in range(4):
            k.drain("v")
            k.on("v", lambda e: e.reduce_max(th[:], cur[:], axis=AX.X))
            if r < 3:
                k.drain("v")
                k.on("v", lambda e: e.tensor_single_scalar(msk[:], cur[:], th[:], ALU.is_ge))
                k.drain("v")
                k.on("v", lambda e: e.scalar_tensor_tensor(cur[:], msk[:], -1e30, cur[:],
                                                           ALU.mult, ALU.add))
        k.drain("v")
        k.on("v", lambda e: e.tensor_single_scalar(msk[:], bias_[:], th[:], ALU.is_ge))
        k.drain("v")
        k.on("v", lambda e: e.tensor_mul(wsel[:], scor[:], msk[:]))
        k.drain("v")
        k.on("v", lambda e: e.reduce_sum(wsum[:], wsel[:], axis=AX.X))
        k.drain("v")
        k.on("v", lambda e: e.reciprocal(winv[:], wsum[:]))
        k.drain("v")
        k.on("v", lambda e, ti=ti: e.tensor_scalar(combA[:, ti, 0:16], wsel[:], winv[:], 2.5,
                                                   ALU.mult, ALU.mult))
    for ti in range(16):
        k.wait("t", "v", "g")
        k.on("t", lambda e, ti=ti: e.transpose(pct[0:17, :], combA[:, ti, :], identf[:]))
        k.wait("v", "t")
        k.on("v", lambda e, ti=ti: e.tensor_copy(ctSB[0:17, ti * 128:(ti + 1) * 128], pct[0:17, :]))
    k.wait("g", "v")
    k.dma("g", lambda e: e.dma_start(out=ctDR[0:17, :], in_=ctSB[0:17, :]))

    pairs = [(2 * j, 2 * j + 1) for j in range(8)] + [(16, None)]
    for (e0, e1) in pairs:
        exps = [e0] if e1 is None else [e0, e1]
        for ii, ee in enumerate(exps):
            for ci in range(8):
                buf = ci % 2
                k.wait("y", "g")
                k.dma("y", lambda e, ee=ee, ci=ci, buf=buf: e.dma_start(
                    out=wst[:, 0],
                    in_=w13c[ee, :, ci * 128:(ci + 1) * 128].rearrange("(c p) f -> p c f", p=128)))
                k.wait("g", "dy", "t")
                k.on("g", lambda e, ci=ci, buf=buf: e.tensor_copy(
                    wb13[:, :, ci * 128:(ci + 1) * 128], wst[:, 0]))
            for ci in range(8):
                buf = ci % 2
                k.wait("y", "g")
                k.dma("y", lambda e, ee=ee, ci=ci, buf=buf: e.dma_start(
                    out=wst[:, 0, 0:4],
                    in_=w2c[ee, :, ci * 128:(ci + 1) * 128].rearrange("(c p) f -> p c f", p=128)))
                k.wait("g", "dy", "t")
                k.on("g", lambda e, ii=ii, ci=ci, buf=buf: e.tensor_copy(
                    wb2[:, ii, :, ci * 128:(ci + 1) * 128], wst[:, 0, 0:4]))
            k.wait("y", "t", "dg")
            k.dma("y", lambda e, ee=ee: e.dma_start(out=cbrow[:], in_=ctDR[ee:ee + 1, :]))
            k.wait("t", "dy", "v", "g")
            for tj in range(4):
                k.wait("t", "v")
                k.on("t", lambda e, tj=tj: e.matmul(pbc[:], onesb[:], cbrow[:, tj * 512:(tj + 1) * 512],
                                                    start=True, stop=True))
                k.wait("v", "t")
                k.on("v", lambda e, ii=ii, tj=tj: e.tensor_copy(
                    cb2[:, ii, tj * 512:(tj + 1) * 512], pbc[:]))
            for fch in range(8):
                for tj in range(4):
                    buf = (fch * 4 + tj) % 2
                    k.wait("t", "v", "s", "g")
                    for dch in range(8):
                        k.on("t", lambda e, fch=fch, tj=tj, dch=dch, buf=buf: e.matmul(
                            p1[:, buf], wb13[:, dch, fch * 128:(fch + 1) * 128],
                            lnT[:, dch, tj * 512:(tj + 1) * 512],
                            start=(dch == 0), stop=(dch == 7)))
                    if fch < 4:
                        k.wait("s", "t")
                        k.on("s", lambda e, fch=fch, tj=tj, buf=buf: e.activation(
                            g4[:, fch, tj * 512:(tj + 1) * 512], p1[:, buf], AF.Silu))
                    else:
                        k.wait("v", "t")
                        k.on("v", lambda e, ii=ii, fch=fch, tj=tj, buf=buf: e.tensor_tensor(
                            h2[:, ii, fch - 4, tj * 512:(tj + 1) * 512], p1[:, buf],
                            cb2[:, ii, tj * 512:(tj + 1) * 512], ALU.mult))
            k.wait("v", "s")
            k.on("v", lambda e, ii=ii: e.tensor_tensor(h2[:, ii], h2[:, ii], g4[:], ALU.mult))
        for tsub in range(16):
            for dh in range(2):
                buf = (tsub * 2 + dh) % 2
                k.wait("t", "v")
                nmm = len(exps) * 4
                cnt = 0
                for ii in range(len(exps)):
                    for fch in range(4):
                        cnt += 1
                        k.on("t", lambda e, ii=ii, fch=fch, tsub=tsub, dh=dh, buf=buf,
                             st=(cnt == 1), sp=(cnt == nmm): e.matmul(
                            p2[:, buf], h2[:, ii, fch, tsub * 128:(tsub + 1) * 128],
                            wb2[:, ii, fch, dh * 512:(dh + 1) * 512], start=st, stop=sp))
                k.wait("v", "t")
                k.on("v", lambda e, tsub=tsub, dh=dh, buf=buf: e.tensor_tensor(
                    x2s[:, tsub, dh * 512:(dh + 1) * 512], x2s[:, tsub, dh * 512:(dh + 1) * 512],
                    p2[:, buf], ALU.add))

    k.wait("g", "v")
    for ti in range(16):
        k.dma("g", lambda e, ti=ti: e.dma_start(out=x2[ti * 128:(ti + 1) * 128, :], in_=x2s[:, ti]))
    k.wait("g", "dg")
    return k.finalize()



def build_S1():
    k = K("S1")
    nc = k.nc
    x2 = nc.dram_tensor("x2", [TPC, D], F32, kind="ExternalInput").ap()
    wcat = nc.dram_tensor("wcat", [D, 5120], F32, kind="ExternalInput").ap()
    waT = nc.dram_tensor("waT", [D, 16], F32, kind="ExternalInput").ap()
    dtb = nc.dram_tensor("dtb", [1, 16], F32, kind="ExternalInput").ap()
    alog = nc.dram_tensor("alog", [1, 16], F32, kind="ExternalInput").ap()
    qn = nc.dram_tensor("qn", [TPC, D], BF16, kind="ExternalOutput").ap()
    kn = nc.dram_tensor("kn", [TPC, D], BF16, kind="ExternalOutput").ap()
    vv = nc.dram_tensor("vv", [TPC, D], BF16, kind="ExternalOutput").ap()
    bb = nc.dram_tensor("bb", [TPC, D], BF16, kind="ExternalOutput").ap()
    gsil = nc.dram_tensor("gsil", [TPC, D], F32, kind="ExternalOutput").ap()
    la = nc.dram_tensor("la", [TPC, 16], F32, kind="ExternalOutput").ap()

    k.setup_chans()
    xt = k.sbuf("xt", [128, 2, D], F32)
    lnb = k.sbuf("lnb", [128, D], BF16)
    lnT = k.sbuf("lnT", [128, 8, TPC], BF16)
    wb = k.sbuf("wb", [128, 8, 5120], BF16)
    wab = k.sbuf("wab", [128, 8, 16], BF16)
    wst = k.sbuf("wst", [128, 8, 128], F32)
    qraw = k.sbuf("qraw", [128, D], F32)
    outf = k.sbuf("outf", [128, D], F32)
    outb = k.sbuf("outb", [128, D], BF16)
    sqs = k.sbuf("sqs", [128, 16], F32)
    rst = k.sbuf("rst", [128, 16], F32)
    t16a = k.sbuf("t16a", [128, 16], F32)
    t16b = k.sbuf("t16b", [128, 16], F32)
    t16c = k.sbuf("t16c", [128, 16], F32)
    dtbB = k.sbuf("dtbB", [128, 16], F32)
    negAB = k.sbuf("negAB", [128, 16], F32)
    row16 = k.sbuf("row16", [1, 16], F32)
    onesf = k.sbuf("onesf", [1, 128], F32)
    onec = emit_const_tile(k, "onec", [128, 1], 1.0)
    ident = emit_identity(k)
    lnsc = make_ln_scratch(k, "ln")
    ptr = k.psum("ptr", [128, 128], BF16)
    pmm = k.psum("pmm", [128, 2, 512], F32)
    p16 = k.psum("p16", [128, 16], F32)

    k.on("g", lambda e: e.memset(onesf[:], 1.0))

    k.dma("y", lambda e: e.dma_start(out=row16[:], in_=dtb))
    k.wait("t", "dy", "g")
    k.on("t", lambda e: e.matmul(p16[:], onesf[:], row16[:], start=True, stop=True))
    k.wait("v", "t")
    k.on("v", lambda e: e.tensor_copy(dtbB[:], p16[:]))
    k.wait("y", "t")
    k.dma("y", lambda e: e.dma_start(out=row16[:], in_=alog))
    k.wait("s", "dy", "v")
    k.on("s", lambda e: e.activation(row16[:], row16[:], AF.Exp))
    k.wait("t", "s", "v")
    k.on("t", lambda e: e.matmul(p16[:], onesf[:], row16[:], start=True, stop=True))
    k.wait("v", "t")
    k.on("v", lambda e: e.tensor_scalar_mul(negAB[:], p16[:], -1.0))

    for ci in range(40):
        k.wait("y", "g")
        k.dma("y", lambda e, ci=ci: e.dma_start(
            out=wst[:], in_=wcat[:, ci * 128:(ci + 1) * 128].rearrange("(c p) f -> p c f", p=128)))
        k.wait("g", "dy")
        k.on("g", lambda e, ci=ci: e.tensor_copy(wb[:, :, ci * 128:(ci + 1) * 128], wst[:]))
    k.wait("y", "g")
    k.dma("y", lambda e: e.dma_start(out=wst[:, :, 0:16], in_=waT.rearrange("(c p) f -> p c f", p=128)))
    k.wait("g", "dy")
    k.on("g", lambda e: e.tensor_copy(wab[:], wst[:, :, 0:16]))

    for ti in range(16):
        buf = ti % 2
        k.wait("y", "s", "v")
        k.dma("y", lambda e, ti=ti, buf=buf: e.dma_start(out=xt[:, buf], in_=x2[ti * 128:(ti + 1) * 128, :]))
        k.wait("v", "dy")
        k.wait("s", "dy")
        k.wait("v", "t")
        emit_ln_tile(k, xt[:, buf], lnb[:], lnsc)
        for ch in range(8):
            k.wait("t", "v")
            k.on("t", lambda e, ch=ch: e.transpose(ptr[:], lnb[:, ch * 128:(ch + 1) * 128], ident[:]))
            k.wait("v", "t")
            k.on("v", lambda e, ti=ti, ch=ch: e.tensor_copy(lnT[:, ch, ti * 128:(ti + 1) * 128], ptr[:]))

    def seg_norm_store(ti, dst, scale):
        k.wait("s", "v")
        k.on("s", lambda e: e.activation(lnsc["sq"][:], qraw[:], AF.Square))
        k.wait("v", "s")
        k.on("v", lambda e: e.reduce_sum(sqs[:].rearrange("p (h one) -> p h one", one=1),
                                         lnsc["sq"][:].rearrange("p (h d) -> p h d", h=16), axis=AX.X))
        k.wait("s", "v")
        k.on("s", lambda e: e.activation(rst[:], sqs[:], AF.Sqrt))
        k.wait("v", "s")
        k.on("v", lambda e: e.reciprocal(rst[:], rst[:]))
        if scale != 1.0:
            k.drain("v")
            k.on("v", lambda e: e.tensor_scalar_mul(rst[:], rst[:], scale))
        k.drain("v")
        k.on("v", lambda e: e.tensor_tensor(
            outb[:].rearrange("p (h d) -> p h d", h=16),
            qraw[:].rearrange("p (h d) -> p h d", h=16),
            rst[:].unsqueeze(2).broadcast_to([128, 16, 64]), ALU.mult))
        k.wait("g", "v")
        k.dma("g", lambda e, ti=ti, dst=dst: e.dma_start(out=dst[ti * 128:(ti + 1) * 128, :], in_=outb[:]))

    for ti in range(16):
        for p in range(5):
            for fh in range(2):
                buf = fh
                k.wait("t", "v", "s")
                for ch in range(8):
                    k.on("t", lambda e, ti=ti, p=p, fh=fh, ch=ch, buf=buf: e.matmul(
                        pmm[:, buf], lnT[:, ch, ti * 128:(ti + 1) * 128],
                        wb[:, ch, p * 1024 + fh * 512: p * 1024 + (fh + 1) * 512],
                        start=(ch == 0), stop=(ch == 7)))
            if p in (0, 1):
                k.wait("v", "t", "g")
                k.on("v", lambda e: e.tensor_copy(qraw[:, 0:512], pmm[:, 0]))
                k.on("v", lambda e: e.tensor_copy(qraw[:, 512:1024], pmm[:, 1]))
                seg_norm_store(ti, qn if p == 0 else kn, 0.125 if p == 0 else 1.0)
            elif p == 2:
                k.wait("v", "t", "g")
                k.on("v", lambda e: e.tensor_copy(outb[:, 0:512], pmm[:, 0]))
                k.on("v", lambda e: e.tensor_copy(outb[:, 512:1024], pmm[:, 1]))
                k.wait("g", "v")
                k.dma("g", lambda e, ti=ti: e.dma_start(out=vv[ti * 128:(ti + 1) * 128, :], in_=outb[:]))
            elif p == 3:
                k.wait("s", "t", "dg")
                k.on("s", lambda e: e.activation(outf[:, 0:512], pmm[:, 0], AF.Silu))
                k.on("s", lambda e: e.activation(outf[:, 512:1024], pmm[:, 1], AF.Silu))
                k.wait("g", "s")
                k.dma("g", lambda e, ti=ti: e.dma_start(out=gsil[ti * 128:(ti + 1) * 128, :], in_=outf[:]))
            else:
                k.wait("s", "t", "dg")
                k.on("s", lambda e: e.activation(outb[:, 0:512], pmm[:, 0], AF.Sigmoid))
                k.on("s", lambda e: e.activation(outb[:, 512:1024], pmm[:, 1], AF.Sigmoid))
                k.wait("g", "s")
                k.dma("g", lambda e, ti=ti: e.dma_start(out=bb[ti * 128:(ti + 1) * 128, :], in_=outb[:]))
        k.wait("t", "v", "s")
        for ch in range(8):
            k.on("t", lambda e, ti=ti, ch=ch: e.matmul(
                p16[:], lnT[:, ch, ti * 128:(ti + 1) * 128], wab[:, ch, :],
                start=(ch == 0), stop=(ch == 7)))
        k.wait("v", "t")
        k.on("v", lambda e: e.tensor_add(t16a[:], p16[:], dtbB[:]))
        k.wait("s", "v")
        k.on("s", lambda e: e.activation(t16b[:], t16a[:], AF.Abs))
        k.drain("s")
        k.on("s", lambda e: e.activation(t16b[:], t16b[:], AF.Exp, scale=-1.0))
        k.drain("s")
        k.on("s", lambda e: e.activation(t16b[:], t16b[:], AF.Ln, bias=onec[:]))
        k.wait("v", "s")
        k.on("v", lambda e: e.tensor_scalar_max(t16c[:], t16a[:], 0.0))
        k.drain("v")
        k.on("v", lambda e: e.tensor_add(t16c[:], t16c[:], t16b[:]))
        k.drain("v")
        k.on("v", lambda e: e.tensor_mul(t16c[:], t16c[:], negAB[:]))
        k.wait("g", "v")
        k.dma("g", lambda e, ti=ti: e.dma_start(out=la[ti * 128:(ti + 1) * 128, :], in_=t16c[:]))

    k.wait("g", "dg")
    return k.finalize()



def build_S2(nblocks=256, parts=9):
    k = K("S2")
    nc = k.nc
    kcT_d = nc.dram_tensor("kcT", [64, 32768], BF16, kind="ExternalInput").ap()
    qcT_d = nc.dram_tensor("qcT", [64, 32768], BF16, kind="ExternalInput").ap()
    ktok_d = nc.dram_tensor("ktok", [32768, 64], BF16, kind="ExternalInput").ap()
    vtok_d = nc.dram_tensor("vtok", [32768, 64], BF16, kind="ExternalInput").ap()
    btok_d = nc.dram_tensor("btok", [32768, 64], BF16, kind="ExternalInput").ap()
    lasc_d = nc.dram_tensor("lasc", [1024, 32], F32, kind="ExternalInput").ap()
    ys_d = nc.dram_tensor("ys", [32768, 64], F32, kind="ExternalOutput").ap()
    gdr = nc.dram_tensor("gdr", [1024, 32], F32, kind="Internal").ap()

    k.setup_chans()
    kcT = k.sbuf("kcT_s", [64, 2, 128], BF16)
    qcT = k.sbuf("qcT_s", [64, 2, 128], BF16)
    vA = k.sbuf("vA", [128, 256, 64], BF16)
    bA = k.sbuf("bA", [128, 256, 64], BF16)
    ktk = k.sbuf("ktk", [128, 2, 64], BF16)
    ktf = k.sbuf("ktf", [128, 64], BF16)
    ktnT = k.sbuf("ktnT", [64, 2, 128], BF16)
    khat = k.sbuf("khat", [128, 2, 4, 64], BF16)
    rqm = k.sbuf("rqm", [128, 4, 256], F32)
    gcolP = k.sbuf("gcolP", [128, 256], F32)
    lasb = k.sbuf("lasb", [128, 8, 32], F32)
    ones32 = k.sbuf("ones32", [128, 32], F32)
    Gsb = k.sbuf("Gsb", [128, 8, 32], F32)
    grow = k.sbuf("grow", [1, 2, 128], F32)
    Gcol = k.sbuf("Gcol", [128, 256], F32)
    ngcol = k.sbuf("ngcol", [128, 256], F32)
    rqc = k.sbuf("rqc", [128, 256], F32)
    gend = k.sbuf("gend", [128, 256], F32)
    gcB = k.sbuf("gcB", [64, 4, 256], F32)
    selE = k.sbuf("selE", [128, 128], F32)
    sel2 = k.sbuf("sel2", [128, 4, 64], F32)
    onesf = k.sbuf("onesf", [1, 128], F32)
    identb = emit_identity(k, "identb")
    identf = k.sbuf("identf", [128, 128], F32)
    mA = k.sbuf("mA", [128, 128], F32)
    mM = k.sbuf("mM", [128, 128], F32)
    dti = k.sbuf("dti", [128, 128], F32)
    ea = k.sbuf("ea", [128, 128], F32)
    em = k.sbuf("em", [128, 128], F32)
    gkn = k.sbuf("gkn", [128, 128], BF16)
    gqk = k.sbuf("gqk", [128, 128], BF16)
    anegT = k.sbuf("anegT", [128, 128], BF16)
    mT = k.sbuf("mT", [128, 128], BF16)
    u_sb = k.sbuf("u_sb", [128, 64], BF16)
    rb = k.sbuf("rb", [128, 64], BF16)
    zf = k.sbuf("zf", [64, 4, 64], F32)
    zt = k.sbuf("zt", [64, 4, 64], F32)
    zb = k.sbuf("zb", [64, 4, 64], BF16)
    zlo = k.sbuf("zlo", [64, 4, 64], BF16)
    tq = k.sbuf("tq", [128, 64], F32)
    yst = k.sbuf("yst", [128, 64], F32)

    pg = k.psum("pg", [128, 512], F32)
    pg2 = k.psum("pg2", [128, 512], F32)
    pgb = k.psum("pgb", [128, 512], F32)
    ptrK = k.psum("ptrK", [64, 128], BF16)
    pmA = k.psum("pmA", [128, 512], F32)
    pmB = k.psum("pmB", [128, 512], F32)
    pz = k.psum("pz", [64, 256], F32)
    py = k.psum("py", [128, 128], F32)

    k.on("g", lambda e: e.memset(onesf[:], 1.0))
    k.on("g", lambda e: e.memset(ones32[:], 1.0))
    k.on("g", lambda e: e.memset(identf[:], 0.0))
    k.on("g", lambda e: e.affine_select(out=identf[:], in_=identf[:], compare_op=ALU.not_equal,
                                        fill=1.0, base=0, pattern=[[-1, 128]], channel_multiplier=1))
    k.on("g", lambda e: e.memset(zf[:], 0.0))
    k.on("g", lambda e: e.memset(zb[:], 0.0))
    k.on("g", lambda e: e.memset(zlo[:], 0.0))
    k.on("g", lambda e: e.memset(selE[:], 0.0))
    k.on("g", lambda e: e.affine_select(out=selE[:], in_=selE[:], compare_op=ALU.not_equal,
                                        fill=1.0, base=-31, pattern=[[-32, 4], [0, 32]],
                                        channel_multiplier=1))
    k.on("g", lambda e: e.memset(sel2[:], 0.0))
    for s in range(4):
        k.on("g", lambda e, s=s: e.affine_select(out=sel2[:, s], in_=sel2[:, s],
                                                 compare_op=ALU.not_equal, fill=1.0,
                                                 base=-32 * s, pattern=[[0, 64]],
                                                 channel_multiplier=1))
    for (mm_, strict) in [(mA, True), (mM, False)]:
        k.on("g", lambda e, mm_=mm_: e.memset(mm_[:], 0.0))
        k.on("g", lambda e, mm_=mm_: e.affine_select(out=mm_[:], in_=mm_[:], compare_op=ALU.is_ge,
                                                     fill=NEG, base=0, pattern=[[-32, 4], [0, 32]],
                                                     channel_multiplier=1))
        k.on("g", lambda e, mm_=mm_: e.affine_select(out=mm_[:], in_=mm_[:], compare_op=ALU.is_ge,
                                                     fill=NEG, base=31, pattern=[[32, 4], [0, 32]],
                                                     channel_multiplier=-1))
        k.on("g", lambda e, mm_=mm_, strict=strict: e.affine_select(
            out=mm_[:], in_=mm_[:], compare_op=ALU.is_ge, fill=NEG,
            base=(-1 if strict else 0), pattern=[[32, 4], [1, 32]], channel_multiplier=-1))

    k.dma("y", lambda e: e.dma_start(out=vA[:], in_=vtok_d.rearrange("(i p) e -> p i e", p=128)))
    k.dma("y", lambda e: e.dma_start(out=bA[:], in_=btok_d.rearrange("(i p) e -> p i e", p=128)))
    k.dma("y", lambda e: e.dma_start(out=lasb[:], in_=lasc_d.rearrange("(r p) t -> p r t", p=128)))
    k.wait("v", "dy", "g")
    for r in range(8):
        k.on("v", lambda e, r=r: e.tensor_tensor_scan(Gsb[:, r], ones32[:], lasb[:, r],
                                                      0.0, ALU.mult, ALU.add))
    k.wait("g", "v")
    k.dma("g", lambda e: e.dma_start(out=gdr.rearrange("(r p) t -> p r t", p=128), in_=Gsb[:]))
    k.wait("y", "dg")
    for r in range(8):
        s, hi = r // 2, r % 2
        k.wait("t", "v", "g")
        k.on("t", lambda e, r=r: e.transpose(pg[0:32, 0:128], Gsb[:, r], identf[:]))
        k.wait("v", "t")
        k.on("v", lambda e, s=s, hi=hi: e.tensor_copy(
            Gcol[32 * s:32 * (s + 1), hi * 128:(hi + 1) * 128], pg[0:32, 0:128]))
    k.wait("s", "v")
    k.on("s", lambda e: e.activation(gcolP[:], Gcol[:], AF.Exp))
    k.wait("v", "s")
    k.on("v", lambda e: e.tensor_scalar_mul(ngcol[:], gcolP[:], -1.0))
    k.wait("t", "v", "g")
    k.on("t", lambda e: e.matmul(pg[:, 0:256], selE[:], Gcol[:], start=True, stop=True))
    k.wait("v", "t")
    k.on("v", lambda e: e.tensor_copy(gend[:], pg[:, 0:256]))
    k.drain("v")
    k.on("v", lambda e: e.tensor_sub(rqc[:], gend[:], Gcol[:]))
    k.wait("s", "v")
    k.on("s", lambda e: e.activation(rqc[:], rqc[:], AF.Exp))
    k.wait("g", "s")
    for s in range(4):
        k.on("g", lambda e, s=s: e.affine_select(out=rqm[:, s], in_=rqc[:], compare_op=ALU.is_ge,
                                                 fill=0.0, base=-32 * s, pattern=[[0, 256]],
                                                 channel_multiplier=1))
        k.on("g", lambda e, s=s: e.affine_select(out=rqm[:, s], in_=rqm[:, s], compare_op=ALU.is_ge,
                                                 fill=0.0, base=32 * s + 31, pattern=[[0, 256]],
                                                 channel_multiplier=-1))
    for s in range(4):
        k.wait("t", "v", "s")
        k.on("t", lambda e, s=s: e.matmul(pz[:, :], sel2[:, s], gend[:], start=True, stop=True))
        k.wait("s", "t")
        k.on("s", lambda e, s=s: e.activation(gcB[:, s], pz[:, :], AF.Exp))

    k.wait("t", "s", "v", "g", "dy")
    for i in range(nblocks):
        ring = i % 2
        k.wait("y", "v", "t")
        k.dma("y", lambda e, i=i, ring=ring: e.dma_start(
            out=ktk[:, ring], in_=ktok_d[128 * i:128 * (i + 1), :]))
        k.dma("y", lambda e, i=i, ring=ring: e.dma_start(
            out=grow[:, ring].rearrange("o (s t) -> o s t", s=4),
            in_=gdr.rearrange("(s i) t -> i s t", s=4)[i].unsqueeze(0)))
        k.dma("y", lambda e, i=i, ring=ring: e.dma_start(
            out=kcT[:, ring], in_=kcT_d[:, 128 * i:128 * (i + 1)]))
        k.dma("y", lambda e, i=i, ring=ring: e.dma_start(
            out=qcT[:, ring], in_=qcT_d[:, 128 * i:128 * (i + 1)]))
        k.wait("v", "dy", "t")
        for s in range(4):
            k.on("v", lambda e, i=i, ring=ring, s=s: e.tensor_tensor(
                khat[:, ring, s], ktk[:, ring], rqm[:, s, i:i + 1].to_broadcast([128, 64]), ALU.mult))
        k.on("v", lambda e, i=i, ring=ring: e.tensor_tensor(
            ktf[:], ktk[:, ring], ngcol[:, i:i + 1].to_broadcast([128, 64]), ALU.mult))
        k.wait("t", "v", "s")
        k.on("t", lambda e: e.transpose(ptrK[:, :], ktf[:], identb[:]))
        k.wait("v", "t")
        k.on("v", lambda e, ring=ring: e.tensor_copy(ktnT[:, ring], ptrK[:, :]))

        if parts < 2:
            continue
        k.on("t", lambda e, i=i, ring=ring: e.matmul(pg[:, 0:128], kcT[:, ring],
                                          kcT[:, ring], start=True, stop=True))
        k.on("t", lambda e, i=i, ring=ring: e.matmul(pg2[:, 0:128], kcT[:, ring],
                                          qcT[:, ring], start=True, stop=True))
        k.on("t", lambda e, i=i, ring=ring: e.matmul(pgb[:, 0:128], onesf[:], grow[:, ring],
                                          start=True, stop=True))
        k.wait("v", "t")
        k.on("v", lambda e, i=i: e.tensor_tensor(dti[:], pgb[:, 0:128],
                                                 Gcol[:, i:i + 1].to_broadcast([128, 128]),
                                                 ALU.subtract))
        k.wait("s", "v", "g")
        k.on("s", lambda e: e.activation(gkn[:], pg[:, 0:128], AF.Copy, scale=-1.0))
        k.on("s", lambda e: e.activation(gqk[:], pg2[:, 0:128], AF.Copy))
        k.wait("v", "s")
        k.on("v", lambda e: e.tensor_add(ea[:], dti[:], mA[:]))
        k.on("v", lambda e: e.tensor_add(em[:], dti[:], mM[:]))
        k.wait("s", "v")
        k.on("s", lambda e: e.activation(ea[:], ea[:], AF.Exp))
        k.on("s", lambda e: e.activation(em[:], em[:], AF.Exp))
        k.wait("v", "s")
        k.on("v", lambda e: e.tensor_mul(anegT[:], gkn[:], ea[:]))
        k.on("v", lambda e: e.tensor_mul(mT[:], gqk[:], em[:]))

        if parts < 3:
            continue
        pm = [pmA, pmB]
        k.wait("t", "v")
        k.on("t", lambda e, i=i: e.matmul(pm[0][:, 0:64], identb[:], vA[:, i], start=True, stop=False))
        for s in range(4):
            k.on("t", lambda e, i=i, s=s, ring=ring: e.matmul(
                pm[0][32 * s:32 * (s + 1), 0:64], ktnT[:, ring, 32 * s:32 * (s + 1)],
                zb[:, s], start=False, stop=False, tile_position=(0, 32 * s)))
            k.on("t", lambda e, i=i, s=s, ring=ring: e.matmul(
                pm[0][32 * s:32 * (s + 1), 0:64], ktnT[:, ring, 32 * s:32 * (s + 1)],
                zlo[:, s], start=False, stop=(s == 3), tile_position=(0, 32 * s)))
        k.wait("v", "t")
        k.on("v", lambda e: e.tensor_copy(u_sb[:], pm[0][:, 0:64]))
        k.drain("v")
        k.on("v", lambda e, i=i: e.tensor_tensor(rb[:], pm[0][:, 0:64], bA[:, i], ALU.mult))
        for mi in range(1, 32):
            pcur = pm[mi % 2]
            k.wait("t", "v")
            k.on("t", lambda e, pcur=pcur: e.matmul(pcur[:, 0:64], anegT[:], rb[:],
                                                    start=True, stop=False))
            k.on("t", lambda e, pcur=pcur: e.matmul(pcur[:, 0:64], identb[:], u_sb[:],
                                                    start=False, stop=True))
            k.wait("v", "t")
            k.on("v", lambda e, i=i, pcur=pcur: e.tensor_tensor(rb[:], pcur[:, 0:64],
                                                                bA[:, i], ALU.mult))
        if parts < 4:
            continue
        k.wait("t", "v")
        k.on("t", lambda e: e.matmul(py[:, 0:64], mT[:], rb[:], start=True, stop=True))
        for s in range(4):
            k.on("t", lambda e, i=i, s=s, ring=ring: e.matmul(
                py[32 * s:32 * (s + 1), 64:128], qcT[:, ring, 32 * s:32 * (s + 1)],
                zb[:, s], start=True, stop=False, tile_position=(0, 32 * s)))
            k.on("t", lambda e, i=i, s=s, ring=ring: e.matmul(
                py[32 * s:32 * (s + 1), 64:128], qcT[:, ring, 32 * s:32 * (s + 1)],
                zlo[:, s], start=False, stop=True, tile_position=(0, 32 * s)))
        if parts < 5:
            continue
        k.wait("t", "v")
        for s in range(4):
            k.on("t", lambda e, i=i, s=s, ring=ring: e.matmul(
                pz[:, 64 * s:64 * (s + 1)], khat[:, ring, s],
                rb[:], start=True, stop=(s == 3)))
        if parts < 6:
            continue
        k.wait("v", "t")
        k.on("v", lambda e, i=i: e.tensor_single_scalar(tq[:], py[:, 64:128],
                                                        gcolP[:, i:i + 1], ALU.mult))
        k.drain("v")
        k.on("v", lambda e: e.tensor_add(yst[:], tq[:], py[:, 0:64]))
        k.on("v", lambda e, i=i: e.tensor_tensor(zt[:], zf[:],
                                                 gcB[:, :, i:i + 1].broadcast_to([64, 4, 64]),
                                                 ALU.mult))
        k.drain("v")
        k.on("v", lambda e: e.tensor_add(zf[:].rearrange("p s e -> p (s e)"),
                                         zt[:].rearrange("p s e -> p (s e)"), pz[:, :]))
        k.drain("v")
        k.on("v", lambda e: e.tensor_copy(zb[:], zf[:]))
        k.drain("v")
        k.on("v", lambda e: e.tensor_sub(zlo[:], zf[:], zb[:]))
        k.wait("g", "v")
        k.dma("g", lambda e, i=i: e.dma_start(out=ys_d[128 * i:128 * (i + 1), :], in_=yst[:]))

    k.wait("g", "dg")
    return k.finalize()



def build_S3():
    k = K("S3")
    nc = k.nc
    x2 = nc.dram_tensor("x2", [TPC, D], F32, kind="ExternalInput").ap()
    ysd = nc.dram_tensor("ysd", [TPC, D], F32, kind="ExternalInput").ap()
    gsil = nc.dram_tensor("gsil", [TPC, D], F32, kind="ExternalInput").ap()
    nwd = nc.dram_tensor("nwd", [1, 64], F32, kind="ExternalInput").ap()
    woT = nc.dram_tensor("woT", [D, D], F32, kind="ExternalInput").ap()
    bo = nc.dram_tensor("bo", [1, D], F32, kind="ExternalInput").ap()
    wgT = nc.dram_tensor("wgT", [D, 8], F32, kind="ExternalInput").ap()
    x3 = nc.dram_tensor("x3", [TPC, D], F32, kind="ExternalOutput").ap()
    ln4 = nc.dram_tensor("ln4", [TPC, D], BF16, kind="ExternalOutput").ap()
    smo = nc.dram_tensor("smo", [TPC, 8], F32, kind="ExternalOutput").ap()

    k.setup_chans()
    yst = k.sbuf("yst", [128, D], F32)
    gst = k.sbuf("gst", [128, D], F32)
    x2t = k.sbuf("x2t", [128, D], F32)
    x3t = k.sbuf("x3t", [128, D], F32)
    ctxb = k.sbuf("ctxb", [128, D], BF16)
    ctxT = k.sbuf("ctxT", [128, 8, 128], BF16)
    ln4b = k.sbuf("ln4b", [128, D], BF16)
    ln4T = k.sbuf("ln4T", [128, 8, 128], BF16)
    wob = k.sbuf("wob", [128, 8, D], BF16)
    wgb8 = k.sbuf("wgb8", [128, 8, 8], BF16)
    wst = k.sbuf("wst", [128, 8, 128], F32)
    nwB = k.sbuf("nwB", [128, 64], F32)
    boB = k.sbuf("boB", [128, D], F32)
    row64 = k.sbuf("row64", [1, 64], F32)
    rowD = k.sbuf("rowD", [1, D], F32)
    onesf = k.sbuf("onesf", [1, 128], F32)
    sqs = k.sbuf("sqs", [128, 16], F32)
    rst = k.sbuf("rst", [128, 16], F32)
    m8 = k.sbuf("m8", [128, 1], F32)
    nm8 = k.sbuf("nm8", [128, 1], F32)
    l8 = k.sbuf("l8", [128, 1], F32)
    li8 = k.sbuf("li8", [128, 1], F32)
    sm8 = k.sbuf("sm8", [128, 8], F32)
    eps6 = emit_const_tile(k, "eps6", [128, 1], 1e-6)
    ident = emit_identity(k)
    lnsc = make_ln_scratch(k, "ln")
    ptr = k.psum("ptr", [128, 128], BF16)
    pmm = k.psum("pmm", [128, 2, 512], F32)
    pbc = k.psum("pbc", [128, 512], F32)
    p8 = k.psum("p8", [128, 8], F32)

    k.on("g", lambda e: e.memset(onesf[:], 1.0))
    k.dma("y", lambda e: e.dma_start(out=row64[:], in_=nwd))
    k.wait("t", "dy", "g")
    k.on("t", lambda e: e.matmul(pbc[:, 0:64], onesf[:], row64[:], start=True, stop=True))
    k.wait("v", "t")
    k.on("v", lambda e: e.tensor_copy(nwB[:], pbc[:, 0:64]))
    k.dma("y", lambda e: e.dma_start(out=rowD[:], in_=bo))
    k.wait("t", "dy", "v")
    for hh in range(2):
        k.on("t", lambda e, hh=hh: e.matmul(pbc[:], onesf[:], rowD[:, hh * 512:(hh + 1) * 512],
                                            start=True, stop=True))
        k.wait("v", "t")
        k.on("v", lambda e, hh=hh: e.tensor_copy(boB[:, hh * 512:(hh + 1) * 512], pbc[:]))
        k.wait("t", "v")
    for ci in range(8):
        k.wait("y", "g")
        k.dma("y", lambda e, ci=ci: e.dma_start(
            out=wst[:], in_=woT[:, ci * 128:(ci + 1) * 128].rearrange("(c p) f -> p c f", p=128)))
        k.wait("g", "dy")
        k.on("g", lambda e, ci=ci: e.tensor_copy(wob[:, :, ci * 128:(ci + 1) * 128], wst[:]))
    k.wait("y", "g")
    k.dma("y", lambda e: e.dma_start(out=wst[:, :, 0:8], in_=wgT.rearrange("(c p) f -> p c f", p=128)))
    k.wait("g", "dy")
    k.on("g", lambda e: e.tensor_copy(wgb8[:], wst[:, :, 0:8]))

    for ti in range(16):
        sl = slice(ti * 128, (ti + 1) * 128)
        k.wait("y", "v", "s")
        k.dma("y", lambda e, sl=sl: e.dma_start(out=yst[:], in_=ysd[sl, :]))
        k.dma("y", lambda e, sl=sl: e.dma_start(out=gst[:], in_=gsil[sl, :]))
        k.dma("y", lambda e, sl=sl: e.dma_start(out=x2t[:], in_=x2[sl, :]))
        k.wait("s", "dy")
        k.on("s", lambda e: e.activation(lnsc["sq"][:], yst[:], AF.Square))
        k.wait("v", "s", "dy")
        k.on("v", lambda e: e.reduce_sum(sqs[:].rearrange("p (h one) -> p h one", one=1),
                                         lnsc["sq"][:].rearrange("p (h d) -> p h d", h=16), axis=AX.X))
        k.drain("v")
        k.on("v", lambda e: e.tensor_scalar_mul(sqs[:], sqs[:], 1.0 / 64))
        k.wait("s", "v")
        k.on("s", lambda e: e.activation(rst[:], sqs[:], AF.Sqrt, bias=eps6[:]))
        k.wait("v", "s")
        k.on("v", lambda e: e.reciprocal(rst[:], rst[:]))
        k.drain("v")
        k.on("v", lambda e: e.tensor_tensor(
            lnsc["sq"][:].rearrange("p (h d) -> p h d", h=16),
            yst[:].rearrange("p (h d) -> p h d", h=16),
            rst[:].unsqueeze(2).broadcast_to([128, 16, 64]), ALU.mult))
        k.drain("v")
        k.on("v", lambda e: e.tensor_tensor(
            lnsc["sq"][:].rearrange("p (h d) -> p h d", h=16),
            lnsc["sq"][:].rearrange("p (h d) -> p h d", h=16),
            nwB[:].unsqueeze(1).broadcast_to([128, 16, 64]), ALU.mult))
        k.drain("v")
        k.on("v", lambda e: e.tensor_mul(ctxb[:], lnsc["sq"][:], gst[:]))
        for ch in range(8):
            k.wait("t", "v")
            k.on("t", lambda e, ch=ch: e.transpose(ptr[:], ctxb[:, ch * 128:(ch + 1) * 128], ident[:]))
            k.wait("v", "t")
            k.on("v", lambda e, ch=ch: e.tensor_copy(ctxT[:, ch], ptr[:]))
        for hh in range(2):
            k.wait("t", "v")
            for ch in range(8):
                k.on("t", lambda e, ch=ch, hh=hh: e.matmul(
                    pmm[:, hh], ctxT[:, ch], wob[:, ch, hh * 512:(hh + 1) * 512],
                    start=(ch == 0), stop=(ch == 7)))
            k.wait("v", "t")
            k.on("v", lambda e, hh=hh: e.tensor_add(x3t[:, hh * 512:(hh + 1) * 512],
                                                    pmm[:, hh], boB[:, hh * 512:(hh + 1) * 512]))
            k.drain("v")
            k.on("v", lambda e, hh=hh: e.tensor_add(x3t[:, hh * 512:(hh + 1) * 512],
                                                    x3t[:, hh * 512:(hh + 1) * 512],
                                                    x2t[:, hh * 512:(hh + 1) * 512]))
        k.wait("g", "v")
        k.dma("g", lambda e, sl=sl: e.dma_start(out=x3[sl, :], in_=x3t[:]))
        k.wait("v", "t")
        k.wait("s", "v")
        emit_ln_tile(k, x3t[:], ln4b[:], lnsc)
        k.wait("g", "v")
        k.dma("g", lambda e, sl=sl: e.dma_start(out=ln4[sl, :], in_=ln4b[:]))
        for ch in range(8):
            k.wait("t", "v")
            k.on("t", lambda e, ch=ch: e.transpose(ptr[:], ln4b[:, ch * 128:(ch + 1) * 128], ident[:]))
            k.wait("v", "t")
            k.on("v", lambda e, ch=ch: e.tensor_copy(ln4T[:, ch], ptr[:]))
        k.wait("t", "v")
        for ch in range(8):
            k.on("t", lambda e, ch=ch: e.matmul(p8[:], ln4T[:, ch], wgb8[:, ch, :],
                                                start=(ch == 0), stop=(ch == 7)))
        k.wait("v", "t")
        k.on("v", lambda e: e.reduce_max(m8[:], p8[:], axis=AX.X))
        k.drain("v")
        k.on("v", lambda e: e.tensor_scalar_mul(nm8[:], m8[:], -1.0))
        k.wait("s", "v")
        k.on("s", lambda e: e.activation(sm8[:], p8[:], AF.Exp, bias=nm8[:], accum_out=l8[:]))
        k.wait("v", "s")
        k.on("v", lambda e: e.reciprocal(li8[:], l8[:]))
        k.drain("v")
        k.on("v", lambda e: e.tensor_single_scalar(sm8[:], sm8[:], li8[:], ALU.mult))
        k.wait("g", "v")
        k.dma("g", lambda e, sl=sl: e.dma_start(out=smo[sl, :], in_=sm8[:]))

    k.wait("g", "dg")
    return k.finalize()



def build_M1a():
    k = K("M1a")
    nc = k.nc
    smc = nc.dram_tensor("smc", [2, 128, 64], F32, kind="ExternalInput").ap()
    dstv = nc.dram_tensor("dstv", [2, 128, 64], I32, kind="ExternalOutput").ap()

    k.setup_chans()
    smt = k.sbuf("smt", [128, 64], F32)
    lo = k.sbuf("lo", [1, 1], I32)
    hi = k.sbuf("hi", [1, 1], I32)
    su = k.sbuf("su", [1, 1], I32)
    mid = k.sbuf("mid", [1, 1], I32)
    dlt = k.sbuf("dlt", [1, 1], I32)
    dlt2 = k.sbuf("dlt2", [1, 1], I32)
    ci = k.sbuf("ci", [1, 1], I32)
    omi = k.sbuf("omi", [1, 1], I32)
    c01 = k.sbuf("c01", [1, 1], F32)
    msk = k.sbuf("msk", [128, 64], F32)
    red = k.sbuf("red", [128, 1], F32)
    rsc = k.sbuf("rsc", [128, 64], F32)
    d1 = k.sbuf("d1", [128, 64], F32)
    desti = k.sbuf("desti", [128, 64], I32)
    ones64 = k.sbuf("ones64", [128, 64], F32)
    onescol = k.sbuf("onescol", [128, 1], F32)
    onesf = k.sbuf("onesf", [1, 128], F32)
    trils = k.sbuf("trils", [128, 128], F32)
    pth = k.psum("pth", [128, 512], F32)

    k.on("g", lambda e: e.memset(ones64[:], 1.0))
    k.on("g", lambda e: e.memset(onescol[:], 1.0))
    k.on("g", lambda e: e.memset(onesf[:], 1.0))
    k.on("g", lambda e: e.memset(trils[:], 1.0))
    k.on("g", lambda e: e.affine_select(out=trils[:], in_=trils[:], compare_op=ALU.is_ge,
                                        fill=0.0, base=-1, pattern=[[-1, 128]],
                                        channel_multiplier=1))
    for b in range(2):
        k.wait("y", "t", "v", "g")
        k.dma("y", lambda e, b=b: e.dma_start(out=smt[:], in_=smc[b]))
        k.wait("g", "t", "v")
        k.on("g", lambda e: e.memset(lo[:], 0))
        k.on("g", lambda e: e.memset(hi[:], 0x3F800000))
        k.wait("v", "dy", "g")
        for it in range(31):
            k.drain("v")
            k.on("v", lambda e: e.tensor_add(su[:], lo[:], hi[:]))
            k.drain("v")
            k.on("v", lambda e: e.tensor_single_scalar(mid[:], su[:], 1, ALU.arith_shift_right))
            k.wait("t", "v")
            k.on("t", lambda e: e.matmul(pth[:, 0:1], onesf[:], mid[:].bitcast(F32),
                                         start=True, stop=True))
            k.wait("v", "t")
            k.on("v", lambda e: e.tensor_single_scalar(msk[:], smt[:], pth[:, 0:1], ALU.is_ge))
            k.drain("v")
            k.on("v", lambda e: e.reduce_sum(red[:], msk[:], axis=AX.X))
            k.wait("t", "v")
            k.on("t", lambda e: e.matmul(pth[0:1, 1:2], red[:], onescol[:],
                                         start=True, stop=True))
            k.wait("v", "t")
            k.on("v", lambda e: e.tensor_single_scalar(c01[:], pth[0:1, 1:2], 512.0, ALU.is_ge))
            k.drain("v")
            k.on("v", lambda e: e.tensor_copy(ci[:], c01[:]))
            k.drain("v")
            k.on("v", lambda e: e.tensor_single_scalar(omi[:], ci[:], 1, ALU.bitwise_xor))
            k.on("v", lambda e: e.tensor_sub(dlt[:], mid[:], lo[:]))
            k.on("v", lambda e: e.tensor_sub(dlt2[:], mid[:], hi[:]))
            k.drain("v")
            k.on("v", lambda e: e.tensor_mul(dlt[:], dlt[:], ci[:]))
            k.on("v", lambda e: e.tensor_mul(dlt2[:], dlt2[:], omi[:]))
            k.drain("v")
            k.on("v", lambda e: e.tensor_add(lo[:], lo[:], dlt[:]))
            k.on("v", lambda e: e.tensor_add(hi[:], hi[:], dlt2[:]))
        k.wait("t", "v")
        k.on("t", lambda e: e.matmul(pth[:, 0:1], onesf[:], lo[:].bitcast(F32), start=True, stop=True))
        k.wait("v", "t")
        k.on("v", lambda e: e.tensor_single_scalar(msk[:], smt[:], pth[:, 0:1], ALU.is_ge))
        k.drain("v")
        k.on("v", lambda e: e.tensor_tensor_scan(rsc[:], ones64[:], msk[:], 0.0, ALU.mult, ALU.add))
        k.drain("v")
        k.on("v", lambda e: e.tensor_copy(red[:], rsc[:, 63:64]))
        k.wait("t", "v")
        k.on("t", lambda e: e.matmul(pth[:, 2:3], trils[:], red[:], start=True, stop=True))
        k.wait("v", "t")
        k.on("v", lambda e: e.tensor_single_scalar(d1[:], rsc[:], pth[:, 2:3], ALU.add))
        k.drain("v")
        k.on("v", lambda e: e.tensor_single_scalar(d1[:], d1[:], 10000.0, ALU.add))
        k.drain("v")
        k.on("v", lambda e: e.scalar_tensor_tensor(d1[:], msk[:], -10001.0, d1[:], ALU.mult, ALU.add))
        k.drain("v")
        k.on("v", lambda e: e.tensor_copy(desti[:], d1[:]))
        k.wait("g", "v")
        k.dma("g", lambda e, b=b: e.dma_start(out=dstv[b], in_=desti[:]))
    k.wait("g", "dg")
    return k.finalize()



def build_M1b():
    k = K("M1b")
    nc = k.nc
    xg2 = nc.dram_tensor("xg2", [2, 512, D], BF16, kind="ExternalInput").ap()
    g52 = nc.dram_tensor("g52", [2, 128, 4], F32, kind="ExternalInput").ap()
    w1 = nc.dram_tensor("w1", [D, 4096], F32, kind="ExternalInput").ap()
    w2 = nc.dram_tensor("w2", [4096, D], F32, kind="ExternalInput").ap()
    b1p = nc.dram_tensor("b1p", [128, 32], F32, kind="ExternalInput").ap()
    b2 = nc.dram_tensor("b2", [1, D], F32, kind="ExternalInput").ap()
    o512 = nc.dram_tensor("o512", [2, 512, D], F32, kind="ExternalOutput").ap()

    k.setup_chans()
    wst2 = k.sbuf("wst2", [128, 8, 256], F32)
    wst = wst2[:, :, 0:128]
    w1b = k.sbuf("w1b", [128, 8, 4096], BF16)
    w2b = k.sbuf("w2b", [128, 32, 512], BF16)
    xg = k.sbuf("xg", [128, 4, D], BF16)
    xgT = k.sbuf("xgT", [128, 8, 512], BF16)
    h1 = k.sbuf("h1", [128, 2, 32, 512], BF16)
    g5 = k.sbuf("g5", [128, 2, 4], F32)
    b2B = k.sbuf("b2B", [128, D], F32)
    b2g = k.sbuf("b2g", [128, 512], F32)
    o5t = k.sbuf("o5t", [128, 512], F32)
    b1s = k.sbuf("b1s", [128, 32], F32)
    rowD = k.sbuf("rowD", [1, D], F32)
    onesf = k.sbuf("onesf", [1, 128], F32)
    ident = emit_identity(k)
    ptr = k.psum("ptr", [128, 128], BF16)
    p1 = k.psum("p1", [128, 2, 512], F32)
    pth = k.psum("pth", [128, 512], F32)

    k.on("g", lambda e: e.memset(onesf[:], 1.0))
    k.dma("y", lambda e: e.dma_start(out=b1s[:], in_=b1p))
    k.dma("y", lambda e: e.dma_start(out=rowD[:], in_=b2))
    for b_ in range(2):
        k.dma("y", lambda e, b_=b_: e.dma_start(out=g5[:, b_],
                                                in_=g52[b_]))
    k.wait("t", "dy", "g")
    for hh in range(2):
        k.wait("t", "v")
        k.on("t", lambda e, hh=hh: e.matmul(pth[:], onesf[:], rowD[:, hh * 512:(hh + 1) * 512],
                                            start=True, stop=True))
        k.wait("v", "t")
        k.on("v", lambda e, hh=hh: e.tensor_copy(b2B[:, hh * 512:(hh + 1) * 512], pth[:]))
    for ci_ in range(32):
        k.wait("y", "g")
        k.dma("y", lambda e, ci_=ci_: e.dma_start(
            out=wst, in_=w1[:, ci_ * 128:(ci_ + 1) * 128].rearrange("(c p) f -> p c f", p=128)))
        k.wait("g", "dy")
        k.on("g", lambda e, ci_=ci_: e.tensor_copy(w1b[:, :, ci_ * 128:(ci_ + 1) * 128], wst))

    for b in range(2):
        k.wait("y", "t")
        k.dma("y", lambda e, b=b: e.dma_start(out=xg[:], in_=xg2[b].rearrange("(c p) d -> p c d", p=128)))
        k.wait("t", "dy", "v")
        for c4 in range(4):
            for ch in range(8):
                k.wait("t", "v")
                k.on("t", lambda e, c4=c4, ch=ch: e.transpose(
                    ptr[:], xg[:, c4, ch * 128:(ch + 1) * 128], ident[:]))
                k.wait("v", "t")
                k.on("v", lambda e, c4=c4, ch=ch: e.tensor_copy(
                    xgT[:, ch, c4 * 128:(c4 + 1) * 128], ptr[:]))
        for fch in range(32):
            buf = fch % 2
            k.wait("t", "s")
            for ch in range(8):
                k.on("t", lambda e, fch=fch, ch=ch, buf=buf: e.matmul(
                    p1[:, buf], w1b[:, ch, fch * 128:(fch + 1) * 128], xgT[:, ch, :],
                    start=(ch == 0), stop=(ch == 7)))
            k.wait("s", "t")
            k.on("s", lambda e, b=b, fch=fch, buf=buf: e.activation(
                h1[:, b, fch, :], p1[:, buf], AF.Gelu_apprx_tanh, bias=b1s[:, fch:fch + 1]))

    for dh in range(2):
        for ci_ in range(4):
            for hh in range(2):
                k.wait("y", "g")
                k.dma("y", lambda e, dh=dh, ci_=ci_, hh=hh: e.dma_start(
                    out=wst2[:], in_=w2[ci_ * 1024:(ci_ + 1) * 1024,
                                        dh * 512 + hh * 256:dh * 512 + (hh + 1) * 256]
                    .rearrange("(c p) f -> p c f", p=128)))
                k.wait("g", "dy", "t")
                k.on("g", lambda e, ci_=ci_, hh=hh: e.tensor_copy(
                    w2b[:, ci_ * 8:(ci_ + 1) * 8, hh * 256:(hh + 1) * 256], wst2[:]))
        for b in range(2):
            for tsub in range(4):
                buf = tsub % 2
                k.wait("t", "v", "g")
                for fch in range(32):
                    k.on("t", lambda e, b=b, tsub=tsub, fch=fch, buf=buf: e.matmul(
                        p1[:, buf], h1[:, b, fch, tsub * 128:(tsub + 1) * 128],
                        w2b[:, fch, :], start=(fch == 0), stop=(fch == 31)))
                k.wait("v", "t")
                k.on("v", lambda e, b=b, tsub=tsub, dh=dh: e.tensor_single_scalar(
                    b2g[:], b2B[:, dh * 512:(dh + 1) * 512], g5[:, b, tsub:tsub + 1], ALU.mult))
                k.on("v", lambda e, b=b, tsub=tsub, buf=buf: e.tensor_single_scalar(
                    o5t[:], p1[:, buf], g5[:, b, tsub:tsub + 1], ALU.mult))
                k.drain("v")
                k.on("v", lambda e: e.tensor_add(o5t[:], o5t[:], b2g[:]))
                k.wait("g", "v")
                k.dma("g", lambda e, b=b, tsub=tsub, dh=dh: e.dma_start(
                    out=o512[b, tsub * 128:(tsub + 1) * 128, dh * 512:(dh + 1) * 512], in_=o5t[:]))

    k.wait("g", "dg")
    return k.finalize()



def build_M2():
    k = K("M2")
    nc = k.nc
    x3 = nc.dram_tensor("x3", [TPC, D], F32, kind="ExternalInput").ap()
    cons = nc.dram_tensor("cons", [8, TPC, D], F32, kind="ExternalInput").ap()
    x4 = nc.dram_tensor("x4", [TPC, D], F32, kind="ExternalOutput").ap()

    k.setup_chans()
    x4t = k.sbuf("x4t", [128, D], F32)
    gbuf = k.sbuf("gbuf", [128, 2, D], F32)
    for ti in range(16):
        k.wait("y", "v", "dg")
        k.dma("y", lambda e, ti=ti: e.dma_start(out=x4t[:], in_=x3[ti * 128:(ti + 1) * 128, :]))
        k.wait("v", "dy")
        for ee in range(8):
            buf = ee % 2
            k.wait("y", "v")
            k.dma("y", lambda e, ti=ti, ee=ee, buf=buf: e.dma_start(
                out=gbuf[:, buf], in_=cons[ee, ti * 128:(ti + 1) * 128, :]))
            k.wait("v", "dy")
            k.on("v", lambda e, buf=buf: e.tensor_add(x4t[:], x4t[:], gbuf[:, buf]))
        k.wait("g", "v")
        k.dma("g", lambda e, ti=ti: e.dma_start(out=x4[ti * 128:(ti + 1) * 128, :], in_=x4t[:]))
    k.wait("g", "dg")
    return k.finalize()



def run(nc, in_maps):
    res = run_bass_kernel_spmd(nc, in_maps, core_ids=list(range(NCORES)))
    return res.results


def _own_tokens(c):
    b = c // 4
    tiles = [4 * i + (c % 4) for i in range(16)]
    idx = np.concatenate([np.arange(t * 128, (t + 1) * 128) for t in tiles])
    return b, idx


def kernel(**inputs):
    x = np.ascontiguousarray(np.asarray(inputs["L0_attn_x"], dtype=np.float32))
    W0 = np.ascontiguousarray(np.asarray(inputs["L0_attn_W"], dtype=np.float32))
    gate_w = np.asarray(inputs["L0_ffn_gate_w"], np.float32)
    gate_b = np.asarray(inputs["L0_ffn_gate_b"], np.float32)
    w13 = np.asarray(inputs["L0_ffn_w13"], np.float32)
    w2_ = np.asarray(inputs["L0_ffn_w2"], np.float32)
    w13s = np.asarray(inputs["L0_ffn_w13_s"], np.float32)
    w2s = np.asarray(inputs["L0_ffn_w2_s"], np.float32)
    Wq = np.asarray(inputs["L1_attn_Wq"], np.float32)
    Wk = np.asarray(inputs["L1_attn_Wk"], np.float32)
    Wv = np.asarray(inputs["L1_attn_Wv"], np.float32)
    Wg = np.asarray(inputs["L1_attn_Wg"], np.float32)
    Wb_ = np.asarray(inputs["L1_attn_Wb"], np.float32)
    Wa = np.asarray(inputs["L1_attn_Wa"], np.float32)
    dtb = np.asarray(inputs["L1_attn_dt_bias"], np.float32)
    alog = np.asarray(inputs["L1_attn_A_log"], np.float32)
    normw = np.asarray(inputs["L1_attn_norm_w"], np.float32)
    Wo = np.asarray(inputs["L1_attn_Wo"], np.float32)
    bo = np.asarray(inputs["L1_attn_bo"], np.float32)
    WgE = np.asarray(inputs["L1_ffn_Wg"], np.float32)
    W1E = np.asarray(inputs["L1_ffn_W1"], np.float32)
    b1E = np.asarray(inputs["L1_ffn_b1"], np.float32)
    W2E = np.asarray(inputs["L1_ffn_W2"], np.float32)
    b2E = np.asarray(inputs["L1_ffn_b2"], np.float32)

    owns = [_own_tokens(c) for c in range(NCORES)]

    nc = build_A1()
    im = [{"x": np.ascontiguousarray(x[b][idx]), "w": W0} for (b, idx) in owns]
    r1 = run(nc, im)
    ktf = [np.empty((D, 8192), BF) for _ in range(2)]
    vf = [np.empty((8192, D), BF) for _ in range(2)]
    for c, (b, idx) in enumerate(owns):
        ktf[b][:, idx] = r1[c]["kt"]
        vf[b][idx, :] = r1[c]["vo"]
    nc = build_A2()
    im = []
    for c, (b, idx) in enumerate(owns):
        cpos = c % 4
        md = np.where(np.arange(512)[None, :] >= 128 * cpos + np.arange(128)[:, None],
                      np.float32(0.0), np.float32(NEG)).astype(np.float32)
        im.append({"x": np.ascontiguousarray(x[b][idx]), "qt": r1[c]["qt"],
                   "ktf": np.ascontiguousarray(ktf[b]), "vf": np.ascontiguousarray(vf[b]),
                   "maskd": md})
    r2 = run(nc, im)
    w13c = np.ascontiguousarray(np.concatenate(
        [np.transpose(w13, (0, 2, 1)), np.ascontiguousarray(w13s.T)[None]], axis=0))
    w2c = np.ascontiguousarray(np.concatenate(
        [np.transpose(w2_, (0, 2, 1)), np.ascontiguousarray(w2s.T)[None]], axis=0))
    gwT = np.ascontiguousarray(gate_w.T)
    nc = build_F1()
    im = [{"x1": r2[c]["x1"], "w13c": w13c, "w2c": w2c, "gwT": gwT,
           "gb": gate_b.reshape(1, 16)} for c in range(NCORES)]
    r3 = run(nc, im)
    wcat = np.ascontiguousarray(np.concatenate([Wq.T, Wk.T, Wv.T, Wg.T, Wb_.T], axis=1))
    waT = np.ascontiguousarray(Wa.T)
    nc = build_S1()
    im = [{"x2": r3[c]["x2"], "wcat": wcat, "waT": waT, "dtb": dtb.reshape(1, 16),
           "alog": alog.reshape(1, 16)} for c in range(NCORES)]
    r4 = run(nc, im)
    def full_b(key, dt):
        out = [np.empty((8192, D), dt) for _ in range(2)]
        for c, (b, idx) in enumerate(owns):
            out[b][idx, :] = r4[c][key]
        return out
    knf = full_b("kn", BF); qnf = full_b("qn", BF); vvf = full_b("vv", BF); bbf = full_b("bb", BF)
    laf = [np.empty((8192, 16), np.float32) for _ in range(2)]
    for c, (b, idx) in enumerate(owns):
        laf[b][idx, :] = r4[c]["la"]
    nc = build_S2()
    im = []
    for c in range(NCORES):
        kcT = np.zeros((64, 256, 4, 32), BF); qcT = np.zeros_like(kcT)
        ktok = np.zeros((256, 4, 32, 64), BF); vtok = np.zeros_like(ktok)
        btok = np.zeros_like(ktok)
        lasc = np.zeros((4, 256, 32), np.float32)
        for s in range(4):
            g = c * 4 + s
            b, h = g // 16, g % 16
            kh = knf[b][:, 64 * h:64 * (h + 1)].reshape(256, 32, 64)
            qh = qnf[b][:, 64 * h:64 * (h + 1)].reshape(256, 32, 64)
            vh = vvf[b][:, 64 * h:64 * (h + 1)].reshape(256, 32, 64)
            bh_ = bbf[b][:, 64 * h:64 * (h + 1)].reshape(256, 32, 64)
            kcT[:, :, s, :] = kh.transpose(2, 0, 1)
            qcT[:, :, s, :] = qh.transpose(2, 0, 1)
            ktok[:, s] = kh; vtok[:, s] = vh; btok[:, s] = bh_
            lasc[s] = laf[b][:, h].reshape(256, 32)
        im.append({"kcT": np.ascontiguousarray(kcT.reshape(64, 32768)),
                   "qcT": np.ascontiguousarray(qcT.reshape(64, 32768)),
                   "ktok": np.ascontiguousarray(ktok.reshape(32768, 64)),
                   "vtok": np.ascontiguousarray(vtok.reshape(32768, 64)),
                   "btok": np.ascontiguousarray(btok.reshape(32768, 64)),
                   "lasc": np.ascontiguousarray(lasc.reshape(1024, 32))})
    r5 = run(nc, im)
    ysf = [np.empty((8192, D), np.float32) for _ in range(2)]
    for c in range(NCORES):
        ysc = r5[c]["ys"].reshape(256, 4, 32, 64)
        for s in range(4):
            g = c * 4 + s
            b, h = g // 16, g % 16
            ysf[b][:, 64 * h:64 * (h + 1)] = ysc[:, s].reshape(8192, 64)
    nc = build_S3()
    im = []
    for c, (b, idx) in enumerate(owns):
        im.append({"x2": r3[c]["x2"], "ysd": np.ascontiguousarray(ysf[b][idx]),
                   "gsil": r4[c]["gsil"], "nwd": normw.reshape(1, 64),
                   "woT": np.ascontiguousarray(Wo.T), "bo": bo.reshape(1, D),
                   "wgT": np.ascontiguousarray(WgE.T)})
    r6 = run(nc, im)
    ln4f = np.empty((2, 8192, D), BF)
    smf = [np.empty((8192, 8), np.float32) for _ in range(2)]
    for c, (b, idx) in enumerate(owns):
        ln4f[b][idx, :] = r6[c]["ln4"]
        smf[b][idx, :] = r6[c]["smo"]
    nc = build_M1a()
    im = []
    for c in range(NCORES):
        smc = np.empty((2, 128, 64), np.float32)
        for b in range(2):
            smc[b] = smf[b][:, c].reshape(64, 128).T
        im.append({"smc": np.ascontiguousarray(smc)})
    r7a = run(nc, im)
    nc = build_M1b()
    im = []
    dflats = {}
    for c in range(NCORES):
        xg2 = np.zeros((2, 512, D), BF)
        g52 = np.zeros((2, 128, 4), np.float32)
        for b in range(2):
            dflat = r7a[c]["dstv"][b].T.reshape(8192)
            sel = dflat < 512
            slots = dflat[sel]
            xg2[b][slots] = ln4f[b][sel]
            g5f = np.zeros(512, np.float32)
            g5f[slots] = smf[b][sel, c]
            g52[b] = g5f.reshape(4, 128).T
            dflats[(c, b)] = dflat
        im.append({"xg2": xg2, "g52": np.ascontiguousarray(g52),
                   "w1": np.ascontiguousarray(W1E[c]),
                   "w2": np.ascontiguousarray(W2E[c]),
                   "b1p": np.ascontiguousarray(b1E[c].reshape(32, 128).T),
                   "b2": b2E[c].reshape(1, D)})
    r7 = run(nc, im)
    nc = build_M2()
    im = []
    for c, (b, idx) in enumerate(owns):
        cons = np.zeros((8, TPC, D), np.float32)
        for e in range(8):
            dflat = dflats[(e, b)]
            dmine = dflat[idx]
            selm = dmine < 512
            cons[e][selm] = r7[e]["o512"][b][dmine[selm]]
        im.append({"x3": r6[c]["x3"], "cons": cons})
    r8 = run(nc, im)
    out = np.empty((2, 8192, D), np.float32)
    for c, (b, idx) in enumerate(owns):
        out[b][idx, :] = r8[c]["x4"]
    return out
